# version 05 (final)
# baseline (speedup 1.0000x reference)
from contextlib import ExitStack
import numpy as np
import concourse.bass as bass
import concourse.mybir as mybir
from concourse.bass_utils import run_bass_kernel_spmd

F32 = mybir.dt.float32
BF16 = mybir.dt.bfloat16
I32 = mybir.dt.int32
AF = mybir.ActivationFunctionType
ALU = mybir.AluOpType
AX = mybir.AxisListType

L = 2048
D = 1024
H = 8
C = 128
NCH = L // C
STRIP = 256
NSTRIP = L // STRIP
HIDN = 4096
NHC = HIDN // 128
FTS = [384] * 5 + [128]
EPS = 1e-5
ENGS = ("tensor", "vector", "scalar", "gpsimd", "sync")


class Ctx:
    def __init__(self, nc, es):
        self.nc = nc
        self.es = es
        self.prog = []
        self.sem = {}
        self.cnt = {}
        self.dstreams = []
        for n in ("t", "v", "s", "g"):
            self.sem[n] = es.enter_context(nc.semaphore("sem_" + n))
            self.cnt[n] = 0

    def dsem(self, name):
        if name not in self.sem:
            self.sem[name] = self.es.enter_context(
                self.nc.semaphore("sem_" + name))
            self.cnt[name] = 0
            self.dstreams.append(name)
        return self.sem[name]

    def emit(self, eng, fn):
        self.prog.append((eng, fn))

    def wait(self, eng, sname, val=None):
        if isinstance(sname, tuple):
            sname, val = sname
        elif isinstance(val, tuple):
            sname, val = val
        if val is None or val <= 0:
            return
        assert val <= self.cnt[sname], (eng, sname, val, self.cnt[sname])
        h = self.sem[sname]
        self.emit(eng, lambda e, h=h, v=val: e.wait_ge(h, v))

    def op(self, eng, build):
        h = self.sem[eng]
        prev = self.cnt[eng]
        ename = {"t": "tensor", "v": "vector", "s": "scalar", "g": "gpsimd"}[eng]
        if prev > 0:
            self.emit(ename, lambda e, h=h, v=prev: e.wait_ge(h, v))
        self.cnt[eng] += 1
        tok = self.cnt[eng]
        def f(e, build=build, h=h):
            build(e).then_inc(h, 1)
        self.emit(ename, f)
        return tok

    def dma(self, queue, out, in_, stream="d0"):
        h = self.dsem(stream)
        self.cnt[stream] += 16
        tok = self.cnt[stream]
        def f(e, out=out, in_=in_, h=h):
            e.dma_start(out=out, in_=in_).then_inc(h, 16)
        self.emit(queue, f)
        return (stream, tok)


class PB:

    def __init__(self, ap):
        self.ap = ap
        self.wr = None
        self.readers = []

    def pre_write(self, ctx):
        for sname, tok in self.readers:
            ctx.wait("tensor", sname, tok)
        self.readers = []


def build_program(limit=None):
    nc = bass.Bass("TRN2", target_bir_lowering=False)

    x_hbm = nc.dram_tensor("x", [L, D], F32, kind="ExternalInput").ap()
    wq_hbm = nc.dram_tensor("wq", [D, D], F32, kind="ExternalInput").ap()
    wk_hbm = nc.dram_tensor("wk", [D, D], F32, kind="ExternalInput").ap()
    wv_hbm = nc.dram_tensor("wv", [D, D], F32, kind="ExternalInput").ap()
    wb_hbm = nc.dram_tensor("wb", [D, H], F32, kind="ExternalInput").ap()
    lnw_hbm = nc.dram_tensor("lnw", [1, D], F32, kind="ExternalInput").ap()
    lnb_hbm = nc.dram_tensor("lnb", [1, D], F32, kind="ExternalInput").ap()
    w1t_hbm = nc.dram_tensor("w1t", [D, HIDN], F32, kind="ExternalInput").ap()
    w2t_hbm = nc.dram_tensor("w2t", [HIDN, D], F32, kind="ExternalInput").ap()
    cw_hbm = nc.dram_tensor("cw", [128, 96], F32, kind="ExternalInput").ap()
    cb_hbm = nc.dram_tensor("cb", [128, 32], F32, kind="ExternalInput").ap()
    out_hbm = nc.dram_tensor("out", [L, D], F32, kind="ExternalOutput").ap()
    x1_hbm = nc.dram_tensor("x1s", [L, D], F32, kind="Internal").ap()
    w1q_hbm = nc.dram_tensor("w1qs", [D, HIDN], BF16, kind="Internal").ap()
    w2q_hbm = nc.dram_tensor("w2qs", [HIDN, D], BF16, kind="Internal").ap()
    xq_hbm = nc.dram_tensor("xqf", [D, L], BF16, kind="Internal").ap()

    es = ExitStack()
    sb = lambda nm, sh, dt: es.enter_context(nc.sbuf_tensor(nm, sh, dt))
    ctx = Ctx(nc, es)

    ones_f = sb("ones_f", [128, 512], F32)
    ones_b4 = sb("ones_b4", [128, 512], BF16)
    ident_b = sb("ident_b", [128, 128], BF16)
    ident4_b = sb("ident4_b", [128, 512], BF16)
    identF = sb("identF", [128, 128], F32)
    tril4 = sb("tril4", [128, 512], BF16)
    triu4 = sb("triu4", [128, 512], BF16)
    lnw_bc = sb("lnw_bc", [128, D], F32)
    lnb_bc = sb("lnb_bc", [128, D], F32)
    ln_row = sb("ln_row", [1, 2 * D], F32)
    onescol_f = sb("onescol_f", [128, 1], F32)
    onesrow_f = sb("onesrow_f", [1, 128], F32)
    onesrow_b = sb("onesrow_b", [1, 128], BF16)
    consts = sb("consts", [128, 2], F32)
    wacc1 = sb("wacc1", [128, 1], F32)
    wacc2 = sb("wacc2", [128, 1], F32)
    wred = sb("wred", [128, 1], F32)
    sw_bc = sb("sw_bc", [128, 4], F32)
    cmrow = sb("cmrow", [1, 4], F32)
    st6 = sb("st6", [128, 12], F32)
    mv2 = sb("mv2", [128, 8], F32)

    cEPS = consts[:, 0:1]
    cHALF = consts[:, 1:2]

    g_ones = ctx.op("g", lambda e: e.memset(ones_f[:], 1.0))
    ctx.op("g", lambda e: e.memset(ones_b4[:], 1.0))
    ctx.op("g", lambda e: e.memset(wacc1[:], 0.0))
    ctx.op("g", lambda e: e.memset(wacc2[:], 0.0))
    ctx.op("g", lambda e: e.memset(consts[:, 0:1], EPS))
    ctx.op("g", lambda e: e.memset(consts[:, 1:2], 128.5))
    ctx.wait("gpsimd", "g", g_ones)
    ctx.op("g", lambda e: e.affine_select(ident4_b[:], ones_b4[:],
           [[0, 4], [-1, 128]], ALU.is_equal, 0.0, base=0, channel_multiplier=1))
    ctx.op("g", lambda e: e.affine_select(identF[:], ones_f[:, 0:128],
           [[-1, 128]], ALU.is_equal, 0.0, base=0, channel_multiplier=1))
    ctx.op("g", lambda e: e.affine_select(tril4[:], ones_b4[:],
           [[0, 4], [-1, 128]], ALU.is_gt, 0.0, base=0, channel_multiplier=1))
    g_masks = ctx.op("g", lambda e: e.affine_select(triu4[:], ones_b4[:],
           [[0, 4], [1, 128]], ALU.is_ge, 0.0, base=0, channel_multiplier=-1))
    ctx.wait("vector", "g", g_masks)
    v_identb = ctx.op("v", lambda e: e.tensor_copy(ident_b[:], ident4_b[:, 0:128]))
    ctx.op("v", lambda e: e.tensor_copy(onescol_f[:], ones_f[:, 0:1]))
    ctx.op("v", lambda e: e.tensor_copy(onesrow_f[0:1, :], ones_f[0:1, 0:128]))
    v_onesrow = ctx.op("v", lambda e: e.tensor_copy(onesrow_b[0:1, :], ones_f[0:1, 0:128]))

    t_ln1 = ctx.dma("sync", ln_row[0:1, 0:D], lnw_hbm, stream="dln1")
    t_lnr = ctx.dma("sync", ln_row[0:1, D:2 * D], lnb_hbm, stream="dln2")

    esA = ExitStack()
    sbA = lambda nm, sh, dt: esA.enter_context(nc.sbuf_tensor(nm, sh, dt))
    wqb = sbA("wqb", [128, 8 * D], BF16)
    wkb = sbA("wkb", [128, 8 * D], BF16)
    wvb = sbA("wvb", [128, 8 * D], BF16)
    wbb = sbA("wbb", [128, 64], BF16)
    XW = 4
    x_win = sbA("x_win", [128, XW * D], F32)
    wqc = sbA("wqc", [128, 2 * 1024], F32)
    xbt = sbA("xbt", [128, D], BF16)
    x_fm = [sbA("x_fm%d" % i, [128, 8 * STRIP], BF16) for i in range(2)]
    q_fm = [sbA("q_fm%d" % i, [128, 8 * STRIP], BF16) for i in range(2)]
    kh_tm = [sbA("kh_tm%d" % i, [128, 8 * STRIP], BF16) for i in range(2)]
    kh_fm = [sbA("kh_fm%d" % i, [128, 8 * STRIP], BF16) for i in range(2)]
    v_tm = [sbA("v_tm%d" % i, [128, 8 * STRIP], BF16) for i in range(2)]
    ktmp = sbA("ktmp", [128, D], F32)
    knrm = sbA("knrm", [128, 3 * H], F32)
    beta_tm = sbA("beta_tm", [128, NCH * H], F32)
    Ab = sbA("Ab", [128, 1024], BF16)
    NPb = {p: sbA("NP%d" % p, [128, 1024], BF16) for p in (1, 2, 4, 8, 16, 32)}
    Apb = [sbA("Ap%d" % i, [128, 1024], BF16) for i in range(2)]
    Rb = [Apb[1], Apb[0]]
    Ttb = sbA("Ttb", [128, 1024], BF16)
    U0b = sbA("U0b", [128, 1024], BF16)
    Ub = sbA("Ub", [128, 1024], BF16)
    Wtb = sbA("Wtb", [128, 1024], BF16)
    Gtb = sbA("Gtb", [128, 1024], BF16)
    bkv = sbA("bkv", [128, 2048], BF16)
    s_sb = sbA("s_sb", [128, 1024], BF16)
    s_f32 = sbA("s_f32", [128, 1024], F32)
    atmp = sbA("atmp", [128, 1024], F32)
    o_tm = sbA("o_tm", [128, D], F32)
    ln_t = sbA("ln_t", [128, D], F32)
    ln_t2 = sbA("ln_t2", [128, D], F32)
    wqm = sbA("wqm", [128, 1024], F32)
    wqa = sbA("wqa", [128, 1024], BF16)
    wqo = sbA("wqo", [128, 2 * 1024], BF16)

    g_ssb0 = ctx.op("g", lambda e: e.memset(s_sb[:], 0.0))
    g_sf0 = ctx.op("g", lambda e: e.memset(s_f32[:], 0.0))

    esAP = ExitStack()
    WK = [PB(esAP.enter_context(nc.psum_tensor("W%d" % i, [128, 512], F32)))
          for i in range(7)]
    X0 = PB(esAP.enter_context(nc.psum_tensor("X0", [128, 1024], BF16)))

    bank_i = [0]
    def nxbank():
        b = WK[bank_i[0] % 7]
        bank_i[0] += 1
        return b

    def mm(bank, outsl, lhsT, rhs, start, stop, fresh):
        if fresh:
            bank.pre_write(ctx)
        tok = ctx.op("t", lambda e, o=bank.ap[outsl], l=lhsT, r=rhs, a=start,
                     b=stop: e.matmul(o, l, r, start=a, stop=b))
        bank.wr = tok
        return tok

    def transp(bank, outsl, in_, ident, fresh, stop=True):
        if fresh:
            bank.pre_write(ctx)
        tok = ctx.op("t", lambda e, o=bank.ap[outsl], i=in_, d=ident, st=fresh,
                     sp=stop: e.matmul(o, i, d, start=st, stop=sp,
                                       is_transpose=True))
        bank.wr = tok
        return tok

    def evac(eng, bank, insl, out):
        ctx.wait({"v": "vector", "s": "scalar"}[eng], "t", bank.wr)
        if eng == "v":
            tok = ctx.op("v", lambda e, i=bank.ap[insl], o=out: e.tensor_copy(o, i))
        else:
            tok = ctx.op("s", lambda e, i=bank.ap[insl], o=out: e.copy(o, i))
        bank.readers.append((eng, tok))
        return tok

    ctx.wait("tensor", t_ln1)
    ctx.wait("tensor", t_lnr)
    ctx.wait("tensor", "v", v_onesrow)
    for rr, dst in ((0, lnw_bc), (1, lnb_bc)):
        for half in range(2):
            b = nxbank()
            b.pre_write(ctx)
            tok = ctx.op("t", lambda e, b=b, rr=rr, half=half: e.matmul(
                b.ap[:, 0:512], onesrow_f[0:1, :],
                ln_row[0:1, rr * D + half * 512: rr * D + (half + 1) * 512],
                start=True, stop=True))
            b.wr = tok
            evac("v", b, np.s_[:, 0:512], dst[:, half * 512:(half + 1) * 512])
    g_ln = ctx.cnt["v"]

    prev_cast = [None, None]
    wcast_last = None
    for wi, (hb, dst) in enumerate([(wq_hbm, wqb), (wk_hbm, wkb), (wv_hbm, wvb)]):
        for dc in range(8):
            slot = (wi * 8 + dc) % 2
            ctx.wait("sync", "v", prev_cast[slot])
            t_l = ctx.dma("sync", wqc[:, slot * D:(slot + 1) * D],
                          hb[dc * 128:(dc + 1) * 128, :],
                          stream="da%d" % slot)
            ctx.wait("vector", t_l)
            wcast_last = ctx.op("v", lambda e, sl=slot, d=dst, dc=dc:
                                e.tensor_copy(d[:, dc * D:(dc + 1) * D],
                                              wqc[:, sl * D:(sl + 1) * D]))
            prev_cast[slot] = wcast_last
    ctx.wait("sync", "v", prev_cast[0])
    t_wb = ctx.dma("sync", wqc[:, 0:64].rearrange("p (c n) -> p c n", n=H),
                   wb_hbm.rearrange("(c p) n -> p c n", p=128), stream="da0")
    ctx.wait("vector", t_wb)
    wcast_last = ctx.op("v", lambda e: e.tensor_copy(wbb[:], wqc[:, 0:64]))

    prev_wload_init = wcast_last
    wam_i = [0]
    wq_list = []
    for dc in range(8):
        wq_list += [("w1", dc, j) for j in range(4)]
    for rc in range(32):
        wq_list += [("w2", rc, 0)]
    wq_i = [0]
    prev_wload = [prev_wload_init, prev_wload_init]
    prev_wout = [None, None]
    cm_tok = [None]

    def sched_absmean():
        i = wam_i[0]
        if i >= 64:
            return False
        wam_i[0] += 1
        if i < 32:
            dc, j = divmod(i, 4)
            src = w1t_hbm[dc * 128:(dc + 1) * 128, j * 1024:(j + 1) * 1024]
            acc = wacc1
        else:
            rc = i - 32
            src = w2t_hbm[rc * 128:(rc + 1) * 128, :]
            acc = wacc2
        slot = i % 2
        ctx.wait("sync", "v", prev_wload[slot])
        t_l = ctx.dma("sync", wqc[:, slot * 1024:(slot + 1) * 1024], src,
                      stream="da%d" % slot)
        ctx.wait("vector", t_l)
        ctx.op("v", lambda e, sl=slot: e.tensor_reduce(
            wred[:], wqc[:, sl * 1024:(sl + 1) * 1024], AX.X, ALU.add,
            apply_absolute_value=True))
        prev_wload[slot] = ctx.op("v", lambda e, a=acc: e.tensor_tensor(
            a[:], a[:], wred[:], ALU.add))
        return True

    def sched_cm():
        b = nxbank()
        b.pre_write(ctx)
        ctx.wait("tensor", "v", prev_wload[0])
        ctx.wait("tensor", "v", prev_wload[1])
        ctx.op("t", lambda e, o=b.ap[0:1, 0:1]: e.matmul(
            o, wacc1[:], onescol_f[:], start=True, stop=False))
        b.wr = ctx.op("t", lambda e, o=b.ap[0:1, 1:2]: e.matmul(
            o, wacc2[:], onescol_f[:], start=False, stop=True))
        ctx.wait("vector", "t", b.wr)
        t1 = ctx.op("v", lambda e, i=b.ap[0:1, 0:2]: e.tensor_scalar(
            cmrow[0:1, 0:2], i, 1.0 / (4096.0 * 1024.0), 1e-5, ALU.mult, ALU.max))
        b.readers.append(("v", t1))
        t2 = ctx.op("v", lambda e: e.reciprocal(cmrow[0:1, 2:4], cmrow[0:1, 0:2]))
        b2 = nxbank()
        b2.pre_write(ctx)
        ctx.wait("tensor", "v", t2)
        tokb = ctx.op("t", lambda e, b2=b2: e.matmul(
            b2.ap[:, 0:4], onesrow_f[0:1, :], cmrow[0:1, 0:4],
            start=True, stop=True))
        b2.wr = tokb
        cm_tok[0] = evac("v", b2, np.s_[:, 0:4], sw_bc[:])

    def sched_wq():
        i = wq_i[0]
        if i >= len(wq_list):
            return False
        wq_i[0] += 1
        which, a, j = wq_list[i]
        if which == "w1":
            src = w1t_hbm[a * 128:(a + 1) * 128, j * 1024:(j + 1) * 1024]
            dst = w1q_hbm[a * 128:(a + 1) * 128, j * 1024:(j + 1) * 1024]
            scol = sw_bc[:, 2:3]
        else:
            src = w2t_hbm[a * 128:(a + 1) * 128, :]
            dst = w2q_hbm[a * 128:(a + 1) * 128, :]
            scol = sw_bc[:, 3:4]
        slot = i % 2
        ctx.wait("gpsimd", "v", prev_wload[slot])
        t_l = ctx.dma("gpsimd", wqc[:, slot * 1024:(slot + 1) * 1024], src,
                      stream="dwq%d" % slot)
        ctx.wait("scalar", t_l)
        ctx.wait("scalar", "v", cm_tok[0])
        t_m = ctx.op("s", lambda e, sc=scol, sl=slot: e.activation(
            wqm[:], wqc[:, sl * 1024:(sl + 1) * 1024], AF.Identity, scale=sc))
        ctx.wait("vector", "s", t_m)
        ctx.op("v", lambda e: e.tensor_scalar(wqa[:], wqm[:], 0.5, None, ALU.is_ge))
        ctx.wait("vector", prev_wout[slot] if prev_wout[slot] else ("t", 0))
        ctx.op("v", lambda e, sl=slot: e.scalar_tensor_tensor(
            wqo[:, sl * 1024:(sl + 1) * 1024], wqm[:], -0.5, wqa[:],
            ALU.is_gt, ALU.add))
        t_q = ctx.op("v", lambda e, sl=slot: e.tensor_scalar(
            wqo[:, sl * 1024:(sl + 1) * 1024], wqo[:, sl * 1024:(sl + 1) * 1024],
            -1.0, None, ALU.add))
        prev_wload[slot] = t_q
        ctx.wait("gpsimd", "v", t_q)
        prev_wout[slot] = ctx.dma("gpsimd", dst,
                                  wqo[:, slot * 1024:(slot + 1) * 1024],
                                  stream="dwo%d" % slot)
        return True

    def sched_background(n):
        for _ in range(n):
            if wam_i[0] < 64:
                sched_absmean()
            elif cm_tok[0] is None:
                sched_cm()
            elif wq_i[0] < len(wq_list):
                sched_wq()

    x_load = [None] * NCH
    x_free = [None] * XW
    prep = [None, None]
    strip_cons = [0, 0]
    prev_ssb = [None, None]
    prev_odma = None
    prev_lnt_dma = None
    o_evac_toks = [None, None]

    for s in range(NSTRIP):
        par = s % 2
        for j in range(2):
            cc = 2 * s + j
            slot = cc % XW
            ctx.wait("sync", "v", x_free[slot])
            x_load[cc] = ctx.dma("sync", x_win[:, slot * D:(slot + 1) * D],
                                 x_hbm[cc * 128:(cc + 1) * 128, :],
                                 stream="dx%d" % slot)
        sched_background(8)

        P = {}
        for j in range(2):
            cc = 2 * s + j
            slot = cc % XW
            ctx.wait("vector", x_load[cc])
            ctx.wait("vector", "t", strip_cons[par])
            t_cast = ctx.op("v", lambda e, sl=slot: e.tensor_copy(
                xbt[:], x_win[:, sl * D:(sl + 1) * D]))
            if s == 0 and j == 0:
                ctx.wait("tensor", "v", v_identb)
                ctx.wait("tensor", "v", wcast_last)
            ctx.wait("tensor", "v", t_cast)
            for dc in range(8):
                transp(X0, np.s_[:, dc * 128:(dc + 1) * 128],
                       xbt[:, dc * 128:(dc + 1) * 128], ident_b[:],
                       fresh=(dc == 0), stop=(dc == 7))
            xfm_out = x_fm[par][:].rearrange("p (d t) -> p d t", t=STRIP)[
                :, :, j * 128:(j + 1) * 128]
            xin = X0.ap[:, 0:1024].rearrange("p (d t) -> p d t", t=128)
            ctx.wait("scalar", "t", X0.wr)
            tok = ctx.op("s", lambda e, o=xfm_out, i=xin: e.copy(o, i))
            X0.readers.append(("s", tok))
            P["x_fm%d" % j] = tok

            KB0, KB1 = nxbank(), nxbank()
            VB0, VB1 = nxbank(), nxbank()
            BB = nxbank()
            ctx.wait("tensor", "s", P["x_fm%d" % j])
            for dc in range(8):
                xfm_c = x_fm[par][:, dc * STRIP + j * 128: dc * STRIP + (j + 1) * 128]
                st = dc == 0
                mm(KB0, np.s_[:, 0:512], xfm_c, wkb[:, dc * D:dc * D + 512],
                   st, dc == 7, fresh=st)
                mm(KB1, np.s_[:, 0:512], xfm_c, wkb[:, dc * D + 512:(dc + 1) * D],
                   st, dc == 7, fresh=st)
                mm(VB0, np.s_[:, 0:512], xfm_c, wvb[:, dc * D:dc * D + 512],
                   st, dc == 7, fresh=st)
                mm(VB1, np.s_[:, 0:512], xfm_c, wvb[:, dc * D + 512:(dc + 1) * D],
                   st, dc == 7, fresh=st)
                mm(BB, np.s_[:, 0:H], xfm_c, wbb[:, dc * H:(dc + 1) * H],
                   st, dc == 7, fresh=st)
            evac("v", KB0, np.s_[:, 0:512], ktmp[:, 0:512])
            ev_k = evac("v", KB1, np.s_[:, 0:512], ktmp[:, 512:1024])
            vout = v_tm[par][:].rearrange("p (h t) -> p h t", t=STRIP)[
                :, :, j * 128:(j + 1) * 128]
            ctx.wait("scalar", "t", VB0.wr)
            tok = ctx.op("s", lambda e, o=vout[:, 0:4, :], i=VB0.ap[:, 0:512]
                         .rearrange("p (h t) -> p h t", t=128):
                         e.activation(o, i, AF.Silu))
            VB0.readers.append(("s", tok))
            ctx.wait("scalar", "t", VB1.wr)
            tok_v = ctx.op("s", lambda e, o=vout[:, 4:8, :], i=VB1.ap[:, 0:512]
                           .rearrange("p (h t) -> p h t", t=128):
                           e.activation(o, i, AF.Silu))
            VB1.readers.append(("s", tok_v))
            P["v%d" % j] = tok_v
            ctx.wait("scalar", "t", BB.wr)
            tok_b = ctx.op("s", lambda e, cc=cc, i=BB.ap[:, 0:H]: e.activation(
                beta_tm[:, cc * H:(cc + 1) * H], i, AF.Sigmoid))
            BB.readers.append(("s", tok_b))
            P["beta%d" % j] = tok_b

            ctx.wait("vector", "v", ev_k)
            ctx.op("v", lambda e: e.tensor_tensor(
                atmp[:], ktmp[:], ktmp[:], ALU.mult))
            ctx.op("v", lambda e: e.tensor_reduce(
                knrm[:, 0:H], atmp[:].rearrange("p (h f) -> p h f", f=128),
                AX.X, ALU.add))
            ctx.wait("scalar", "v", ctx.cnt["v"])
            t_sq = ctx.op("s", lambda e: e.activation(
                knrm[:, H:2 * H], knrm[:, 0:H], AF.Sqrt))
            ctx.wait("vector", "s", t_sq)
            ctx.op("v", lambda e: e.tensor_scalar(
                knrm[:, H:2 * H], knrm[:, H:2 * H], 1e-12, None, ALU.max))
            ctx.op("v", lambda e: e.reciprocal(knrm[:, 2 * H:3 * H],
                                               knrm[:, H:2 * H]))
            kh_toks = []
            for h in range(H):
                kh_toks.append(ctx.op("v", lambda e, h=h, j=j, par=par:
                    e.tensor_scalar(
                    kh_tm[par][:, h * STRIP + j * 128: h * STRIP + (j + 1) * 128],
                    ktmp[:, h * 128:(h + 1) * 128],
                    knrm[:, 2 * H + h:2 * H + h + 1], None, ALU.mult)))
            P["kh_tm%d" % j] = kh_toks[-1]
            ctx.wait("tensor", "v", kh_toks[-1])
            for h in range(H):
                transp(X0, np.s_[:, h * 128:(h + 1) * 128],
                       kh_tm[par][:, h * STRIP + j * 128: h * STRIP + (j + 1) * 128],
                       ident_b[:], fresh=(h == 0), stop=(h == 7))
            khfm_out = kh_fm[par][:].rearrange("p (h t) -> p h t", t=STRIP)[
                :, :, j * 128:(j + 1) * 128]
            ctx.wait("scalar", "t", X0.wr)
            tok = ctx.op("s", lambda e, o=khfm_out,
                         i=X0.ap[:, 0:1024].rearrange("p (h t) -> p h t", t=128):
                         e.copy(o, i))
            X0.readers.append(("s", tok))
            P["kh_fm%d" % j] = tok

        q_toks = []
        for fb in range(8):
            QB = nxbank()
            for dc in range(8):
                mm(QB, np.s_[:, 0:STRIP],
                   wqb[:, dc * D + fb * 128: dc * D + (fb + 1) * 128],
                   x_fm[par][:, dc * STRIP:(dc + 1) * STRIP],
                   dc == 0, dc == 7, fresh=(dc == 0))
            q_toks.append(evac("v", QB, np.s_[:, 0:STRIP],
                               q_fm[par][:, fb * STRIP:(fb + 1) * STRIP]))
        P["q"] = q_toks[-1]

        sched_background(8)

        for j in range(2):
            cc = 2 * s + j

            def khfm_c(h):
                return kh_fm[par][:, h * STRIP + j * 128: h * STRIP + (j + 1) * 128]

            def khtm_c(h):
                return kh_tm[par][:, h * STRIP + j * 128: h * STRIP + (j + 1) * 128]

            def vtm_c(h):
                return v_tm[par][:, h * STRIP + j * 128: h * STRIP + (j + 1) * 128]

            def qfm_c(h):
                return q_fm[par][:, h * STRIP + j * 128: h * STRIP + (j + 1) * 128]

            def bcol(h):
                return beta_tm[:, cc * H + h: cc * H + h + 1]

            ctx.wait("vector", "s", P["beta%d" % j])
            ctx.wait("vector", "s", P["v%d" % j])
            for h in range(H):
                ctx.op("v", lambda e, h=h, i=khtm_c(h), b=bcol(h):
                       e.tensor_scalar(
                    bkv[:, h * 256: h * 256 + 128], i, b, None, ALU.mult))
                ctx.op("v", lambda e, h=h, i=vtm_c(h), b=bcol(h):
                       e.tensor_scalar(
                    bkv[:, h * 256 + 128: (h + 1) * 256], i, b, None, ALU.mult))
            bkv_tok = ctx.cnt["v"]
            if cc == 0:
                ctx.wait("tensor", "g", g_ssb0)

            ctx.wait("tensor", "s", P["kh_fm%d" % j])
            kk = [nxbank(), nxbank()]
            for h in range(H):
                mm(kk[h // 4], np.s_[:, (h % 4) * 128:(h % 4 + 1) * 128],
                   khfm_c(h), khfm_c(h), h % 4 == 0, h % 4 == 3,
                   fresh=(h % 4 == 0))
            for grp in range(2):
                ctx.wait("vector", "t", kk[grp].wr)
                for hh in range(4):
                    h = grp * 4 + hh
                    tok = ctx.op("v", lambda e, h=h,
                                 i=kk[grp].ap[:, hh * 128:(hh + 1) * 128],
                                 b=bcol(h): e.tensor_scalar(
                        atmp[:, h * 128:(h + 1) * 128], i, b, None, ALU.mult))
                    kk[grp].readers.append(("v", tok))
                ctx.op("v", lambda e, g=grp: e.tensor_tensor(
                    Ab[:, g * 512:(g + 1) * 512], atmp[:, g * 512:(g + 1) * 512],
                    tril4[:], ALU.mult))
            a_tok = ctx.cnt["v"]
            ctx.wait("tensor", "v", a_tok)
            for h in range(H):
                transp(X0, np.s_[:, h * 128:(h + 1) * 128],
                       Ab[:, h * 128:(h + 1) * 128], ident_b[:],
                       fresh=(h == 0), stop=(h == 7))
            evac("s", X0, np.s_[:, 0:512], NPb[1][:, 0:512])
            n_tok = evac("s", X0, np.s_[:, 512:1024], NPb[1][:, 512:1024])
            ctx.wait("vector", "s", n_tok)

            acur = Ab
            acur_tok = a_tok
            np_tok = {1: n_tok}
            api = 0
            for lev, p2 in enumerate((2, 4, 8, 16, 32, 64)):
                p = p2 // 2
                ba = [nxbank(), nxbank()]
                ctx.wait("tensor", "v", acur_tok)
                ctx.wait("tensor", "s", np_tok[p])

                for h in range(H):
                    mm(ba[h // 4], np.s_[:, (h % 4) * 128:(h % 4 + 1) * 128],
                       NPb[p][:, h * 128:(h + 1) * 128],
                       acur[:, h * 128:(h + 1) * 128], h % 4 == 0, h % 4 == 3,
                       fresh=(h % 4 == 0))
                if p2 < 64:
                    bn_ = [nxbank(), nxbank()]
                    for h in range(H):
                        mm(bn_[h // 4], np.s_[:, (h % 4) * 128:(h % 4 + 1) * 128],
                           acur[:, h * 128:(h + 1) * 128],
                           NPb[p][:, h * 128:(h + 1) * 128], h % 4 == 0,
                           h % 4 == 3, fresh=(h % 4 == 0))
                    anew = Apb[api % 2]
                    api += 1
                    evac("v", ba[0], np.s_[:, 0:512], anew[:, 0:512])
                    acur_tok = evac("v", ba[1], np.s_[:, 0:512], anew[:, 512:1024])
                    acur = anew
                    evac("s", bn_[0], np.s_[:, 0:512], NPb[p2][:, 0:512])
                    np_tok[p2] = evac("s", bn_[1], np.s_[:, 0:512],
                                      NPb[p2][:, 512:1024])
                else:
                    for grp in range(2):
                        ctx.wait("vector", "t", ba[grp].wr)
                        tok = ctx.op("v", lambda e, g=grp,
                                     i=ba[grp].ap[:, 0:512]: e.tensor_tensor(
                            Rb[0][:, g * 512:(g + 1) * 512],
                            i, ident4_b[:], ALU.add))
                        ba[grp].readers.append(("v", tok))
                    r_tok = ctx.cnt["v"]
            rcur = 0
            for p in (32, 16, 8, 4, 2):
                bb_ = [nxbank(), nxbank()]
                ctx.wait("tensor", "v", r_tok)
                for h in range(H):
                    mm(bb_[h // 4], np.s_[:, (h % 4) * 128:(h % 4 + 1) * 128],
                       NPb[p][:, h * 128:(h + 1) * 128],
                       Rb[rcur][:, h * 128:(h + 1) * 128], h % 4 == 0,
                       h % 4 == 3, fresh=(h % 4 == 0))
                for grp in range(2):
                    ctx.wait("vector", "t", bb_[grp].wr)
                    tok = ctx.op("v", lambda e, g=grp, rc=rcur,
                                 i=bb_[grp].ap[:, 0:512]: e.tensor_tensor(
                        Rb[1 - rc][:, g * 512:(g + 1) * 512],
                        i, Rb[rc][:, g * 512:(g + 1) * 512], ALU.add))
                    bb_[grp].readers.append(("v", tok))
                r_tok = ctx.cnt["v"]
                rcur = 1 - rcur
            bb_ = [nxbank(), nxbank()]
            ctx.wait("tensor", "v", r_tok)
            for h in range(H):
                mm(bb_[h // 4], np.s_[:, (h % 4) * 128:(h % 4 + 1) * 128],
                   NPb[1][:, h * 128:(h + 1) * 128],
                   Rb[rcur][:, h * 128:(h + 1) * 128], h % 4 == 0, h % 4 == 3,
                   fresh=(h % 4 == 0))
            for grp in range(2):
                ctx.wait("vector", "t", bb_[grp].wr)
                tok = ctx.op("v", lambda e, g=grp, rc=rcur,
                             i=bb_[grp].ap[:, 0:512]: e.tensor_tensor(
                    Rb[1 - rc][:, g * 512:(g + 1) * 512],
                    Rb[rc][:, g * 512:(g + 1) * 512],
                    i, ALU.subtract))
                bb_[grp].readers.append(("v", tok))
            T_buf = Rb[1 - rcur]
            T_tok = ctx.cnt["v"]

            ctx.wait("tensor", "v", T_tok)
            for h in range(H):
                transp(X0, np.s_[:, h * 128:(h + 1) * 128],
                       T_buf[:, h * 128:(h + 1) * 128], ident_b[:],
                       fresh=(h == 0), stop=(h == 7))
            evac("s", X0, np.s_[:, 0:512], Ttb[:, 0:512])
            tt_tok = evac("s", X0, np.s_[:, 512:1024], Ttb[:, 512:1024])

            bu = [nxbank(), nxbank()]
            bw = [nxbank(), nxbank()]
            ctx.wait("tensor", "s", tt_tok)
            ctx.wait("tensor", "v", bkv_tok)
            for h in range(H):
                mm(bu[h // 4], np.s_[:, (h % 4) * 128:(h % 4 + 1) * 128],
                   Ttb[:, h * 128:(h + 1) * 128],
                   bkv[:, h * 256 + 128:(h + 1) * 256], h % 4 == 0, h % 4 == 3,
                   fresh=(h % 4 == 0))
            for h in range(H):
                mm(bw[h // 4], np.s_[:, (h % 4) * 128:(h % 4 + 1) * 128],
                   bkv[:, h * 256: h * 256 + 128],
                   Ttb[:, h * 128:(h + 1) * 128], h % 4 == 0, h % 4 == 3,
                   fresh=(h % 4 == 0))
            evac("v", bu[0], np.s_[:, 0:512], U0b[:, 0:512])
            u0_tok = evac("v", bu[1], np.s_[:, 0:512], U0b[:, 512:1024])
            evac("s", bw[0], np.s_[:, 0:512], Wtb[:, 0:512])
            wt_tok = evac("s", bw[1], np.s_[:, 0:512], Wtb[:, 512:1024])

            bus = [nxbank(), nxbank()]
            ctx.wait("tensor", "s", wt_tok)
            if cc > 0:
                ctx.wait("tensor", "v", prev_ssb[0])
            for h in range(H):
                mm(bus[h // 4], np.s_[:, (h % 4) * 128:(h % 4 + 1) * 128],
                   Wtb[:, h * 128:(h + 1) * 128],
                   s_sb[:, h * 128:(h + 1) * 128], h % 4 == 0, h % 4 == 3,
                   fresh=(h % 4 == 0))
            for grp in range(2):
                ctx.wait("vector", "t", bus[grp].wr)
                ctx.wait("vector", "v", u0_tok)
                tok = ctx.op("v", lambda e, g=grp,
                             i=bus[grp].ap[:, 0:512]: e.tensor_tensor(
                    Ub[:, g * 512:(g + 1) * 512], U0b[:, g * 512:(g + 1) * 512],
                    i, ALU.subtract))
                bus[grp].readers.append(("v", tok))
            u_tok = ctx.cnt["v"]

            bg = [nxbank(), nxbank()]
            ctx.wait("tensor", "v", P["q"])
            for h in range(H):
                mm(bg[h // 4], np.s_[:, (h % 4) * 128:(h % 4 + 1) * 128],
                   khfm_c(h), qfm_c(h), h % 4 == 0, h % 4 == 3,
                   fresh=(h % 4 == 0))
            for grp in range(2):
                ctx.wait("vector", "t", bg[grp].wr)
                tok = ctx.op("v", lambda e, g=grp,
                             i=bg[grp].ap[:, 0:512]: e.tensor_tensor(
                    Gtb[:, g * 512:(g + 1) * 512], i,
                    triu4[:], ALU.mult))
                bg[grp].readers.append(("v", tok))
            gt_tok = ctx.cnt["v"]

            bo = [nxbank(), nxbank()]
            ctx.wait("tensor", "v", gt_tok)
            ctx.wait("tensor", "v", u_tok)
            for h in range(H):
                grp, hh = h // 4, h % 4
                mm(bo[grp], np.s_[:, hh * 128:(hh + 1) * 128],
                   qfm_c(h), s_sb[:, h * 128:(h + 1) * 128],
                   hh == 0, False, fresh=(hh == 0))
                mm(bo[grp], np.s_[:, hh * 128:(hh + 1) * 128],
                   Gtb[:, h * 128:(h + 1) * 128], Ub[:, h * 128:(h + 1) * 128],
                   False, hh == 3, fresh=False)

            bs_ = [nxbank(), nxbank()]
            for h in range(H):
                hh = h % 4
                mm(bs_[h // 4], np.s_[:, hh * 128:(hh + 1) * 128],
                   khtm_c(h), Ub[:, h * 128:(h + 1) * 128],
                   hh == 0, hh == 3, fresh=(hh == 0))
            if cc < NCH - 1:
                if cc == 0:
                    ctx.wait("vector", "g", g_sf0)
                for grp in range(2):
                    ctx.wait("vector", "t", bs_[grp].wr)
                    tok = ctx.op("v", lambda e, g=grp,
                                 i=bs_[grp].ap[:, 0:512]: e.tensor_tensor(
                        s_f32[:, g * 512:(g + 1) * 512],
                        s_f32[:, g * 512:(g + 1) * 512], i, ALU.add))
                    bs_[grp].readers.append(("v", tok))
                    tok2 = ctx.op("v", lambda e, g=grp: e.tensor_copy(
                        s_sb[:, g * 512:(g + 1) * 512],
                        s_f32[:, g * 512:(g + 1) * 512]))
                prev_ssb[0] = tok2

            ctx.wait("vector", prev_lnt_dma)
            evac("v", bo[0], np.s_[:, 0:512], o_tm[:, 0:512])
            o_ev = evac("s", bo[1], np.s_[:, 0:512], o_tm[:, 512:1024])
            ctx.wait("vector", "s", o_ev)
            ctx.op("v", lambda e: e.bn_stats(st6[:, 0:6], o_tm[:, 0:512]))
            ctx.op("v", lambda e: e.bn_stats(st6[:, 6:12], o_tm[:, 512:1024]))
            ctx.op("v", lambda e: e.bn_aggr(mv2[:, 0:2], st6[:, 0:12]))
            t_bn = ctx.cnt["v"]
            ctx.wait("scalar", "v", t_bn)
            t_sd = ctx.op("s", lambda e: e.activation(
                mv2[:, 2:3], mv2[:, 1:2], AF.Sqrt, bias=cEPS))
            ctx.wait("vector", "s", t_sd)
            ctx.op("v", lambda e: e.reciprocal(mv2[:, 3:4], mv2[:, 2:3]))
            ctx.op("v", lambda e: e.tensor_scalar(
                ln_t[:], o_tm[:], mv2[:, 0:1], mv2[:, 3:4],
                ALU.subtract, ALU.mult))
            if s == 0 and j == 0:
                ctx.wait("vector", "v", g_ln)
            ctx.op("v", lambda e: e.tensor_tensor(ln_t2[:], ln_t[:], lnw_bc[:],
                                                  ALU.mult))
            ctx.op("v", lambda e: e.tensor_tensor(ln_t[:], ln_t2[:], lnb_bc[:],
                                                  ALU.add))
            slot = cc % XW
            t_res = ctx.op("v", lambda e, sl=slot: e.tensor_tensor(
                ln_t2[:], ln_t[:], x_win[:, sl * D:(sl + 1) * D], ALU.add))
            x_free[slot] = t_res
            ctx.op("v", lambda e: e.bn_stats(st6[:, 0:6], ln_t2[:, 0:512]))
            ctx.op("v", lambda e: e.bn_stats(st6[:, 6:12], ln_t2[:, 512:1024]))
            ctx.op("v", lambda e: e.bn_aggr(mv2[:, 4:6], st6[:, 0:12]))
            t_bn2 = ctx.cnt["v"]
            ctx.wait("scalar", "v", t_bn2)
            t_sd2 = ctx.op("s", lambda e: e.activation(
                mv2[:, 6:7], mv2[:, 5:6], AF.Sqrt, bias=cEPS))
            ctx.wait("vector", "s", t_sd2)
            ctx.op("v", lambda e: e.reciprocal(mv2[:, 7:8], mv2[:, 6:7]))
            t_x1 = ctx.op("v", lambda e: e.tensor_scalar(
                ln_t[:], ln_t2[:], mv2[:, 4:5], mv2[:, 7:8],
                ALU.subtract, ALU.mult))
            ctx.wait("sync", "v", t_x1)
            prev_lnt_dma = ctx.dma("sync", x1_hbm[cc * 128:(cc + 1) * 128, :],
                                   ln_t[:], stream="dx1s")

        strip_cons[par] = ctx.cnt["t"]
        sched_background(6)

    while wam_i[0] < 64 or cm_tok[0] is None or wq_i[0] < len(wq_list):
        sched_background(4)

    toks = {n: ctx.cnt[n]
            for n in ["t", "v", "s", "g"] + list(ctx.dstreams)}
    for ename in ENGS:
        own = {"tensor": "t", "vector": "v", "scalar": "s", "gpsimd": "g",
               "sync": None}[ename]
        for n, v in toks.items():
            if n != own:
                ctx.wait(ename, n, v)

    esAP.close()
    esA.close()

    esB = ExitStack()
    sbB = lambda nm, sh, dt: esB.enter_context(nc.sbuf_tensor(nm, sh, dt))
    w2qt = sbB("w2qt", [128, NHC * 1024], BF16)
    yq = sbB("yq", [128, NHC * 384], BF16)
    x1t = sbB("x1t", [128, 3 * D], F32)
    out_tm = sbB("out_tm", [128, 3 * D], F32)
    cwr = sbB("cwr", [128, 96], F32)
    cbr = sbB("cbr", [128, 32], F32)
    cws = sbB("cws", [128, 96], F32)
    xsc = sbB("xsc", [128, 4], F32)

    esBP = ExitStack()
    HB = [PB(esBP.enter_context(nc.psum_tensor("HB%d" % i, [128, 512], F32)))
          for i in range(2)]
    OB = [PB(esBP.enter_context(nc.psum_tensor("OB%d" % i, [128, 512], F32)))
          for i in range(2)]
    WF = [PB(esBP.enter_context(nc.psum_tensor("WF%d" % i, [128, 512], F32)))
          for i in range(3)]
    XF = PB(esBP.enter_context(nc.psum_tensor("XF", [128, 1024], BF16)))

    t_w2 = ctx.dma("sync", w2qt[:].rearrange("p (a m) -> p a m", m=1024),
                   w2q_hbm.rearrange("(a p) m -> p a m", p=128), stream="dw2q")
    t_cw = ctx.dma("sync", cwr[:], cw_hbm, stream="dcw")
    t_cb = ctx.dma("sync", cbr[:], cb_hbm, stream="dcb")
    ctx.wait("vector", t_cw)
    ctx.wait("vector", t_cb)
    t_cws = ctx.op("v", lambda e: e.tensor_scalar(
        cws[:], cwr[:], sw_bc[:, 0:1], None, ALU.mult))

    esQ = ExitStack()
    sbQ = lambda nm, sh, dt: esQ.enter_context(nc.sbuf_tensor(nm, sh, dt))
    x1q = sbQ("x1q", [128, 2 * D], F32)
    xm = sbQ("xm", [128, D], F32)
    xbi = sbQ("xbi", [128, D], I32)
    xr = sbQ("xr", [128, D], F32)
    xc1 = sbQ("xc1", [128, D], BF16)
    xcs = sbQ("xcs", [128, D], BF16)
    xt2 = sbQ("xt2", [128, D], F32)
    xqs = sbQ("xqs", [128, D], BF16)
    xq_st = sbQ("xq_st", [128, 2 * D], BF16)
    prev_x1q = [None, None]
    prev_xq_dma = [None, None]
    prev_tbi = None
    xq_done = None
    for p in range(NCH):
        slot = p % 2
        ctx.wait("sync", "v", prev_x1q[slot])
        t_l = ctx.dma("sync", x1q[:, slot * D:(slot + 1) * D],
                      x1_hbm[p * 128:(p + 1) * 128, :], stream="dq%d" % slot)
        ctx.wait("vector", t_l)
        ctx.wait("vector", "s", prev_tbi)
        ctx.op("v", lambda e, sl=slot: e.tensor_reduce(
            xsc[:, 0:1], x1q[:, sl * D:(sl + 1) * D], AX.X, ALU.max,
            apply_absolute_value=True))
        ctx.op("v", lambda e: e.tensor_scalar(
            xsc[:, 0:1], xsc[:, 0:1], 1e-5, None, ALU.max))
        ctx.op("v", lambda e: e.reciprocal(xsc[:, 1:2], xsc[:, 0:1]))
        ctx.op("v", lambda e: e.tensor_scalar(
            xsc[:, 1:2], xsc[:, 1:2], 127.0, None, ALU.mult))
        t_pre = ctx.op("v", lambda e: e.tensor_scalar(
            xsc[:, 2:3], xsc[:, 0:1], 1.0 / 127.0, None, ALU.mult))
        ctx.wait("scalar", "v", t_pre)
        t_m = ctx.op("s", lambda e, sl=slot: e.activation(
            xm[:], x1q[:, sl * D:(sl + 1) * D], AF.Identity,
            bias=cHALF, scale=xsc[:, 1:2]))
        t_bi = ctx.op("s", lambda e, sl=slot: e.activation(
            xbi[:], x1q[:, sl * D:(sl + 1) * D], AF.Identity,
            bias=cHALF, scale=xsc[:, 1:2]))
        prev_tbi = t_bi
        ctx.wait("vector", "s", t_bi)
        ctx.op("v", lambda e: e.tensor_tensor(xr[:], xm[:], xbi[:], ALU.subtract))
        ctx.op("v", lambda e: e.tensor_scalar(xc1[:], xr[:], 1.0, None, ALU.is_ge))
        ctx.op("v", lambda e: e.scalar_tensor_tensor(
            xcs[:], xr[:], 0.0, xc1[:], ALU.is_ge, ALU.add))
        ctx.op("v", lambda e: e.tensor_tensor(xt2[:], xbi[:], xcs[:], ALU.add))
        t_q = ctx.op("v", lambda e: e.tensor_scalar(
            xqs[:], xt2[:], -129.0, xsc[:, 2:3], ALU.add, ALU.mult))
        prev_x1q[slot] = t_q
        ctx.wait("tensor", "v", t_q)
        for dc in range(8):
            transp(XF, np.s_[:, dc * 128:(dc + 1) * 128],
                   xqs[:, dc * 128:(dc + 1) * 128], ident_b[:],
                   fresh=(dc == 0), stop=(dc == 7))
        ctx.wait("scalar", "t", XF.wr)
        ctx.wait("scalar", prev_xq_dma[slot] if prev_xq_dma[slot] else ("t", 0))
        tokq = ctx.op("s", lambda e, sl=slot,
                      i=XF.ap[:, 0:1024]:
                      e.copy(xq_st[:, sl * D:(sl + 1) * D], i))
        XF.readers.append(("s", tokq))
        ctx.wait("sync", "s", tokq)
        prev_xq_dma[slot] = ctx.dma(
            "sync",
            xq_hbm.rearrange("(d p) t -> p d t", p=128)[:, :, p * 128:(p + 1) * 128],
            xq_st[:, slot * D:(slot + 1) * D].rearrange("p (d t) -> p d t", t=128),
            stream="dxs%d" % slot)
        xq_done = tokq
    xq_dma_all = [("dxs0", ctx.cnt["dxs0"]), ("dxs1", ctx.cnt["dxs1"])]
    esQ.close()
    esT = ExitStack()
    sbT = lambda nm, sh, dt: esT.enter_context(nc.sbuf_tensor(nm, sh, dt))
    w1s = sbT("w1s", [128, 2 * 4 * 1024], BF16)
    xqt = sbT("xqt", [128, 2 * 8 * 392], BF16)
    h_s = sbT("h_s", [128, 392], F32)
    cv1 = sbT("cv1", [128, 384], F32)
    cv2 = sbT("cv2", [128, 384], F32)
    hm = sbT("hm", [128, 392], F32)
    hbi = sbT("hbi", [128, 392], I32)
    hr = sbT("hr", [128, 392], F32)
    hc1 = sbT("hc1", [128, 392], BF16)
    hcs = sbT("hcs", [128, 392], BF16)
    ht2 = sbT("ht2", [128, 392], F32)
    mxb = sbT("mxb", [128, 2 * 384], BF16)
    mxT = sbT("mxT", [128, 3 * 128], BF16)
    mcol = sbT("mcol", [128, 4], F32)
    mrow = sbT("mrow", [1, 3 * 128], F32)
    srow = sbT("srow", [1, 3 * 128], F32)
    grow = sbT("grow", [1, 3 * 128], F32)
    s2bc = sbT("s2bc", [128, 384], BF16)
    g2bc = sbT("g2bc", [128, 384], F32)
    o2f = sbT("o2f", [128, 384], F32)

    prev_w1 = [None, None]
    prev_xqt = [None, None]
    prev_out_dma = None
    prev_x1t = None
    hs_cons = 0
    yq_cons_t = 0

    t0 = 0
    for ti, TW in enumerate(FTS):
        lo = max(t0 - 1, 0)
        hi = min(t0 + TW + 1, L)
        span = hi - lo
        off = lo - (t0 - 1)

        w1_tok = [None] * 8

        def load_w1(g):
            slot = g % 2
            ctx.wait("sync", "t", prev_w1[slot])
            srcw = w1q_hbm.rearrange("(a p) m -> p a m", p=128)[
                :, :, g * 512:(g + 1) * 512]
            w1_tok[g] = ctx.dma(
                "sync",
                w1s[:, slot * 4096:(slot + 1) * 4096].rearrange(
                    "p (a m) -> p a m", m=512), srcw, stream="dw1_%d" % slot)
        load_w1(0)
        load_w1(1)
        xslot = ti % 2
        ctx.wait("sync", "t", prev_xqt[xslot])
        if ti == 0:
            for st_tok in xq_dma_all:
                ctx.wait("sync", st_tok)
        t_xq = ctx.dma(
            "sync",
            xqt[:, xslot * 8 * 392:xslot * 8 * 392 + 8 * 392].rearrange(
                "p (d t) -> p d t", t=392)[:, :, 0:span],
            xq_hbm.rearrange("(d p) t -> p d t", p=128)[:, :, lo:hi],
            stream="dxq%d" % xslot)
        if prev_out_dma is not None:
            ctx.wait("sync", prev_out_dma)
        ctx.wait("sync", "v", prev_x1t)
        t_x1t = ctx.dma("sync",
                        x1t[:, 0:(TW // 128) * D].rearrange(
                            "p (a m) -> p a m", m=D),
                        x1_hbm[t0:t0 + TW, :].rearrange("(a p) m -> p a m", p=128),
                        stream="dx1t")

        ctx.wait("gpsimd", "t", ctx.cnt["t"])
        ctx.wait("gpsimd", "v", ctx.cnt["v"])
        g_mx = ctx.op("g", lambda e: e.memset(mxb[:], 0.0))
        if ti == 0:
            ctx.wait("gpsimd", "s", xq_done)
            g_hs0 = ctx.op("g", lambda e: e.memset(h_s[:, 0:1], 0.0))
        if ti == len(FTS) - 1:
            g_hs1 = ctx.op("g", lambda e, sp=span, of=off: e.memset(
                h_s[:, sp + of:sp + of + 1], 0.0))

        for hc in range(NHC):
            g = hc // 4
            slot = g % 2
            hcw = hc % 4
            B = HB[hc % 2]
            ctx.wait("tensor", w1_tok[g])
            if hc == 0:
                ctx.wait("tensor", t_xq)
            B.pre_write(ctx)
            for dc in range(8):
                mm(B, np.s_[:, 0:span],
                   w1s[:, slot * 4096 + dc * 512 + hcw * 128:
                       slot * 4096 + dc * 512 + (hcw + 1) * 128],
                   xqt[:, xslot * 8 * 392 + dc * 392:
                       xslot * 8 * 392 + dc * 392 + span],
                   dc == 0, dc == 7, fresh=False)
            if hcw == 3:
                prev_w1[slot] = ctx.cnt["t"]
                if g + 2 < 8:
                    load_w1(g + 2)
            ctx.wait("scalar", "t", B.wr)
            ctx.wait("scalar", "v", hs_cons)
            t_hs = ctx.op("s", lambda e, b=B, sp=span, of=off: e.copy(
                h_s[:, of:of + sp], b.ap[:, 0:sp]))
            B.readers.append(("s", t_hs))
            if ti == 0 and hc == 0:
                ctx.wait("scalar", "g", g_hs0)
            t_c1 = ctx.op("s", lambda e, hc=hc, TW=TW: e.activation(
                cv1[:, 0:TW], h_s[:, 0:TW], AF.Identity,
                bias=cbr[:, hc:hc + 1], scale=cws[:, hc * 3:hc * 3 + 1]))
            if ti == 0 and hc == 0:
                ctx.wait("vector", "g", g_hs0)
                ctx.wait("vector", "v", t_cws)
            if ti == len(FTS) - 1 and hc == 0:
                ctx.wait("vector", "g", g_hs1)
            ctx.wait("vector", "s", t_c1)
            ctx.op("v", lambda e, hc=hc, TW=TW: e.scalar_tensor_tensor(
                cv2[:, 0:TW], h_s[:, 1:TW + 1], cws[:, hc * 3 + 1:hc * 3 + 2],
                cv1[:, 0:TW], ALU.mult, ALU.add))
            t_c3 = ctx.op("v", lambda e, hc=hc, TW=TW: e.scalar_tensor_tensor(
                cv1[:, 0:TW], h_s[:, 2:TW + 2], cws[:, hc * 3 + 2:hc * 3 + 3],
                cv2[:, 0:TW], ALU.mult, ALU.add))
            hs_cons = t_c3
            ctx.wait("scalar", "v", t_c3)
            if ti > 0 and hc == 0:
                ctx.wait("scalar", "t", yq_cons_t)
            t_y = ctx.op("s", lambda e, hc=hc, TW=TW: e.activation(
                yq[:, hc * 384:hc * 384 + TW], cv1[:, 0:TW], AF.Silu))
            ctx.wait("vector", "s", t_y)
            if hc == 0:
                ctx.wait("vector", "g", g_mx)
            ctx.op("v", lambda e, hc=hc, TW=TW: e.tensor_tensor(
                mxb[:, 0:TW], mxb[:, 0:TW], yq[:, hc * 384:hc * 384 + TW],
                ALU.max))
            ctx.op("v", lambda e, hc=hc, TW=TW: e.tensor_tensor(
                mxb[:, 384:384 + TW], mxb[:, 384:384 + TW],
                yq[:, hc * 384:hc * 384 + TW], ALU.min))
        ctx.op("v", lambda e, TW=TW: e.tensor_scalar(
            mxb[:, 384:384 + TW], mxb[:, 384:384 + TW], -1.0, None, ALU.mult))
        mx_tok = ctx.op("v", lambda e, TW=TW: e.tensor_tensor(
            mxb[:, 0:TW], mxb[:, 0:TW], mxb[:, 384:384 + TW], ALU.max))
        prev_xqt[xslot] = ctx.cnt["t"]

        nsub = TW // 128
        ctx.wait("tensor", "v", mx_tok)
        for jj in range(nsub):
            transp(XF, np.s_[:, jj * 128:(jj + 1) * 128],
                   mxb[:, jj * 128:(jj + 1) * 128], ident_b[:],
                   fresh=(jj == 0), stop=(jj == nsub - 1))
        ev_mx = evac("s", XF, np.s_[:, 0:nsub * 128], mxT[:, 0:nsub * 128])
        ctx.wait("vector", "s", ev_mx)
        ctx.op("v", lambda e, n=nsub: e.tensor_reduce(
            mcol[:, 0:n], mxT[:, 0:n * 128].rearrange("p (a f) -> p a f", f=128),
            AX.X, ALU.max))
        t_mc = ctx.op("v", lambda e, n=nsub: e.tensor_scalar(
            mcol[:, 0:n], mcol[:, 0:n], 1e-5, None, ALU.max))
        ctx.wait("tensor", "v", t_mc)
        WFa = WF[2]
        WFa.pre_write(ctx)
        for jj in range(nsub):
            transp(WFa, np.s_[0:1, jj * 128:(jj + 1) * 128],
                   mcol[:, jj:jj + 1], identF[:], fresh=(jj == 0),
                   stop=(jj == nsub - 1))
        ev_mr = evac("v", WFa, np.s_[0:1, 0:nsub * 128], mrow[0:1, 0:nsub * 128])
        ctx.op("v", lambda e, n=nsub: e.reciprocal(srow[0:1, 0:n * 128],
                                                   mrow[0:1, 0:n * 128]))
        ctx.op("v", lambda e, n=nsub: e.tensor_scalar(
            srow[0:1, 0:n * 128], srow[0:1, 0:n * 128], 127.0, None, ALU.mult))
        t_sr = ctx.cnt["v"]
        t_gr = ctx.op("v", lambda e, n=nsub: e.tensor_scalar(
            grow[0:1, 0:n * 128], mrow[0:1, 0:n * 128], cmrow[0:1, 1:2],
            1.0 / 127.0, ALU.mult, ALU.mult))
        ctx.wait("tensor", "v", t_sr)
        WFs = WF[0]
        WFs.pre_write(ctx)
        tokt = ctx.op("t", lambda e, n=nsub: e.matmul(
            WFs.ap[:, 0:n * 128], onesrow_f[0:1, :], srow[0:1, 0:n * 128],
            start=True, stop=True))
        WFs.wr = tokt
        ev_s2 = evac("v", WFs, np.s_[:, 0:nsub * 128], s2bc[:, 0:nsub * 128])
        ctx.wait("tensor", "v", t_gr)
        WFb = WF[1]
        WFb.pre_write(ctx)
        tokt = ctx.op("t", lambda e, n=nsub: e.matmul(
            WFb.ap[:, 0:n * 128], onesrow_f[0:1, :], grow[0:1, 0:n * 128],
            start=True, stop=True))
        WFb.wr = tokt
        ev_g2 = evac("v", WFb, np.s_[:, 0:nsub * 128], g2bc[:, 0:nsub * 128])

        for hc in range(NHC):
            ysl = np.s_[:, hc * 384:hc * 384 + TW]
            if hc == 0:
                ctx.wait("vector", "v", ev_s2)
                ctx.wait("vector", "v", ev_g2)
            ctx.op("v", lambda e, ysl=ysl, TW=TW: e.tensor_tensor(
                hm[:, 0:TW], yq[ysl], s2bc[:, 0:TW], ALU.mult))
            t_hm = ctx.cnt["v"]
            ctx.wait("scalar", "v", t_hm)
            t_bi = ctx.op("s", lambda e, TW=TW: e.activation(
                hbi[:, 0:TW], hm[:, 0:TW], AF.Identity, bias=cHALF))
            ctx.wait("vector", "s", t_bi)
            ctx.op("v", lambda e, TW=TW: e.scalar_tensor_tensor(
                hr[:, 0:TW], hm[:, 0:TW], 128.5, hbi[:, 0:TW],
                ALU.add, ALU.subtract))
            ctx.op("v", lambda e, TW=TW: e.tensor_scalar(
                hc1[:, 0:TW], hr[:, 0:TW], 1.0, None, ALU.is_ge))
            ctx.op("v", lambda e, TW=TW: e.scalar_tensor_tensor(
                hcs[:, 0:TW], hr[:, 0:TW], 0.0, hc1[:, 0:TW],
                ALU.is_ge, ALU.add))
            ctx.op("v", lambda e, TW=TW: e.tensor_tensor(
                ht2[:, 0:TW], hbi[:, 0:TW], hcs[:, 0:TW], ALU.add))
            ctx.op("v", lambda e, ysl=ysl, TW=TW: e.scalar_tensor_tensor(
                yq[ysl], ht2[:, 0:TW], -129.0, g2bc[:, 0:TW],
                ALU.add, ALU.mult))
        hq_tok = ctx.cnt["v"]


        ctx.wait("tensor", "v", hq_tok)
        if ti == 0:
            ctx.wait("tensor", t_w2)
        for db in range(8):
            B = OB[db % 2]
            B.pre_write(ctx)
            for hc in range(NHC):
                mm(B, np.s_[:, 0:TW],
                   w2qt[:, hc * 1024 + db * 128: hc * 1024 + (db + 1) * 128],
                   yq[:, hc * 384:hc * 384 + TW],
                   hc == 0, hc == NHC - 1, fresh=False)
            ev_o = evac("v", B, np.s_[:, 0:TW], o2f[:, 0:TW])
            ctx.wait("tensor", "v", ev_o)
            WFc = WF[2]
            WFc.pre_write(ctx)
            for jj in range(nsub):
                transp(WFc, np.s_[:, jj * 128:(jj + 1) * 128],
                       o2f[:, jj * 128:(jj + 1) * 128], identF[:],
                       fresh=(jj == 0), stop=(jj == nsub - 1))
            ctx.wait("vector", "t", WFc.wr)
            tok = ctx.op("v", lambda e, db=db, n=nsub: e.tensor_copy(
                out_tm[:].rearrange("p (a m) -> p a m", m=D)[
                    :, 0:n, db * 128:(db + 1) * 128],
                WFc.ap[:, 0:n * 128].rearrange("p (a m) -> p a m", m=128)))
            WFc.readers.append(("v", tok))
        o_tok = ctx.cnt["v"]
        yq_cons_t = ctx.cnt["t"]

        ctx.wait("vector", t_x1t)
        for jj in range(nsub):
            xsl = np.s_[:, jj * D:(jj + 1) * D]
            ctx.op("v", lambda e, xsl=xsl: e.tensor_tensor(
                out_tm[xsl], out_tm[xsl], x1t[xsl], ALU.add))
            ctx.op("v", lambda e, xsl=xsl: e.bn_stats(
                st6[:, 0:6], out_tm[:, xsl[1].start:xsl[1].start + 512]))
            ctx.op("v", lambda e, xsl=xsl: e.bn_stats(
                st6[:, 6:12], out_tm[:, xsl[1].start + 512:xsl[1].stop]))
            ctx.op("v", lambda e: e.bn_aggr(mv2[:, 0:2], st6[:, 0:12]))
            t_bn = ctx.cnt["v"]
            ctx.wait("scalar", "v", t_bn)
            t_sd = ctx.op("s", lambda e: e.activation(
                mv2[:, 2:3], mv2[:, 1:2], AF.Sqrt, bias=cEPS))
            ctx.wait("vector", "s", t_sd)
            ctx.op("v", lambda e: e.reciprocal(mv2[:, 3:4], mv2[:, 2:3]))
            t_out = ctx.op("v", lambda e, xsl=xsl: e.tensor_scalar(
                x1t[xsl], out_tm[xsl], mv2[:, 0:1], mv2[:, 3:4],
                ALU.subtract, ALU.mult))
        prev_x1t = t_out
        ctx.wait("sync", "v", t_out)
        prev_out_dma = ctx.dma(
            "sync", out_hbm[t0:t0 + TW, :].rearrange("(a p) m -> p a m", p=128),
            x1t[:, 0:(TW // 128) * D].rearrange("p (a m) -> p a m", m=D),
            stream="dout")
        t0 += TW

    for n in list(ctx.dstreams):
        ctx.wait("sync", n, ctx.cnt[n])

    if limit is not None:
        ctx.prog = ctx.prog[:limit]
    with nc.Block() as block:
        def mk(ename):
            def body(eng):
                for e, fn in ctx.prog:
                    if e == ename:
                        fn(eng)
            return body
        block.tensor(mk("tensor"))
        block.vector(mk("vector"))
        block.scalar(mk("scalar"))
        block.gpsimd(mk("gpsimd"))
        block.sync(mk("sync"))
    esBP.close()
    esT.close()
    esB.close()
    es.close()
    return nc


_NC_CACHE = [None]


def kernel(L0_attn_x, L0_attn_Wq, L0_attn_Wk, L0_attn_Wv, L0_attn_Wb,
           L0_attn_ln_w, L0_attn_ln_b, L0_ffn_w1, L0_ffn_conv_w, L0_ffn_conv_b,
           L0_ffn_w2):
    x = np.ascontiguousarray(np.asarray(L0_attn_x, dtype=np.float32))
    base = {
        "wq": np.ascontiguousarray(np.asarray(L0_attn_Wq, np.float32)),
        "wk": np.ascontiguousarray(np.asarray(L0_attn_Wk, np.float32)),
        "wv": np.ascontiguousarray(np.asarray(L0_attn_Wv, np.float32)),
        "wb": np.ascontiguousarray(np.asarray(L0_attn_Wb, np.float32)),
        "lnw": np.ascontiguousarray(np.asarray(L0_attn_ln_w, np.float32).reshape(1, D)),
        "lnb": np.ascontiguousarray(np.asarray(L0_attn_ln_b, np.float32).reshape(1, D)),
        "w1t": np.ascontiguousarray(np.asarray(L0_ffn_w1, np.float32).T),
        "w2t": np.ascontiguousarray(np.asarray(L0_ffn_w2, np.float32).T),
        "cw": np.ascontiguousarray(np.asarray(L0_ffn_conv_w, np.float32)
                                   .reshape(NHC, 128, 3).transpose(1, 0, 2)
                                   .reshape(128, 96)),
        "cb": np.ascontiguousarray(np.asarray(L0_ffn_conv_b, np.float32)
                                   .reshape(NHC, 128).T),
    }
    if _NC_CACHE[0] is None:
        _NC_CACHE[0] = build_program()
    nc = _NC_CACHE[0]
    in_maps = []
    for b in range(8):
        m = dict(base)
        m["x"] = np.ascontiguousarray(x[b])
        in_maps.append(m)
    res = run_bass_kernel_spmd(nc, in_maps, core_ids=list(range(8)))
    return np.stack([np.asarray(r["out"], np.float32) for r in res.results], axis=0)


if __name__ == "__main__":
    nc = build_program()
    print("instructions:", nc.n_instructions())
